# version 01 (final)
# baseline (speedup 1.0000x reference)
import numpy as np
import ml_dtypes
import contextlib
import concourse.bass as bass
import concourse.mybir as mybir
from concourse.bass_utils import run_bass_kernel_spmd

F32 = mybir.dt.float32
BF16 = mybir.dt.bfloat16
I32 = mybir.dt.int32
I8 = mybir.dt.int8
AF = mybir.ActivationFunctionType
OP = mybir.AluOpType
AX = mybir.AxisListType
BF = ml_dtypes.bfloat16

H = 2048
NH, HD = 16, 128
SQ = 1024
WT = 2048
INTER = 5504
NST = 16
EPS_LN = 1e-5
EPS_RMS = 1e-6
THETA = 10000.0
ENABLE_CC = True

ENGINES = ["sync", "scalar", "vector", "tensor", "gpsimd"]


class Bld:
    def __init__(self, nc, es):
        self.nc = nc
        self.q = {e: [] for e in ENGINES}
        self.sems = {}
        self.cnt = {}
        self.es = es

    def sem(self, name):
        self.sems[name] = self.es.enter_context(self.nc.semaphore(name))
        self.cnt[name] = 0

    def op(self, eng, f, inc=None, inc_by=1):
        h = self.sems.get(inc)
        if inc is not None:
            self.cnt[inc] += inc_by

        def run(e, f=f, h=h, inc=inc, inc_by=inc_by):
            r = f(e)
            if inc is not None:
                r.then_inc(h, inc_by)
        self.q[eng].append(run)
        return self.cnt[inc] if inc is not None else None

    def sw(self, eng, sem):
        self.wait(eng, sem, self.cnt[sem])

    def wait(self, eng, sem, val):
        h = self.sems[sem]
        assert val <= self.cnt[sem], f"wait {sem}>={val} exceeds planned {self.cnt[sem]}"
        self.q[eng].append(lambda e, h=h, val=val: e.wait_ge(h, val))

    def dma(self, eng, out, in_, **kw):
        sem = "d_" + eng
        self.cnt[sem] += 16
        h = self.sems[sem]
        self.q[eng].append(lambda e, o=out, i=in_, h=h, kw=kw: e.dma_start(out=o, in_=i, **kw).then_inc(h, 16))
        return self.cnt[sem]

    def barrier(self):
        h = self.sems["bar"]
        self.cnt["bar"] += 5
        tgt = self.cnt["bar"]
        for e in ENGINES:
            self.q[e].append(lambda en, h=h: en.nop().then_inc(h, 1))
            self.q[e].append(lambda en, h=h, t=tgt: en.wait_ge(h, t))


class Arena:
    def __init__(self, t, n):
        self.t, self.n, self.off, self.marks = t, n, 0, []

    def al(self, n, tag=""):
        assert self.off + n <= self.n, f"arena overflow {tag}: {self.off}+{n}>{self.n}"
        ap = self.t[:, self.off:self.off + n]
        self.off += n
        return ap

    def al3(self, c, m, tag=""):
        return self.al(c * m, tag).rearrange("p (c m) -> p c m", m=m)

    def alf(self, n_f32, tag=""):
        return self.al(2 * n_f32, tag).bitcast(F32)

    def ali8(self, n_i8, tag=""):
        n = (n_i8 + 1) // 2
        return self.al(n, tag).bitcast(I8)

    def push(self):
        self.marks.append(self.off)

    def pop(self):
        self.off = self.marks.pop()


def build_program(debug=False):
    es = contextlib.ExitStack()
    nc = bass.Bass("TRN2", target_bir_lowering=False)

    def din(name, shape, dt=F32):
        return nc.dram_tensor(name, shape, dt, kind="ExternalInput")

    def dint(name, shape, dt=F32):
        if debug:
            return nc.dram_tensor(name, shape, dt, kind="ExternalOutput")
        return nc.dram_tensor(name, shape, dt)

    x_own = din("x_own", [SQ, H])
    x_halo = din("x_halo", [SQ, H])
    wqkv = din("wqkv", [H, 3 * H])
    wo = din("wo", [H, H])
    win = din("win", [H, 2 * H])
    win_bt = din("win_bt", [H, NST])
    win_dl = din("win_dl", [H, 1])
    bias_v = din("bias_v", [128, 16])
    bias_xc = din("bias_xc", [128, 16])
    bias_bt = din("bias_bt", [NST, 1])
    bias_dl = din("bias_dl", [1, 1])
    gq_in = din("gq", [128, 16])
    gk_in = din("gk", [128, 16])
    cosw_in = din("cosw", [128, WT])
    sinw_in = din("sinw", [128, WT])
    maskmul = din("maskmul", [8, 128, 1152], I8)
    aarr_in = din("aarr", [128, 256])
    carr_in = din("carr", [128, 256])
    convw_in = din("convw", [128, 64])
    bparam_in = din("bparam", [NST, 1])
    lutv = din("lutv", [1, 16])
    gidxT = din("gidxT", [256, INTER], I32)
    uidxT = din("uidxT", [256, INTER], I32)
    didxT = din("didxT", [688, H], I32)
    sel_a1 = din("sel_a1", [128, 8])
    sel_ap = din("sel_ap", [128, 8])
    sel_b = din("sel_b", [128, 8])
    sel_tl = din("sel_tl", [128, 8])
    y_out = nc.dram_tensor("y_out", [SQ, H], F32, kind="ExternalOutput")

    id_bf_d = nc.inline_tensor(np.eye(128, dtype=BF), name="id_bf")
    id_f_d = nc.inline_tensor(np.eye(128, dtype=np.float32), name="id_f")
    ones_d = nc.inline_tensor(np.ones((128, 1), dtype=np.float32), name="ones_d")
    onesr_d = nc.inline_tensor(np.ones((1, 128), dtype=np.float32), name="onesr_d")
    _ohbc = np.zeros((16, 16 * 128), dtype=np.float32)
    for n in range(16):
        _ohbc[n, n * 128:(n + 1) * 128] = 1
    ohbc_d = nc.inline_tensor(_ohbc, name="ohbc")
    cvals = np.array([EPS_LN, EPS_RMS, 1.0, -0.5, 1.0 / H, 1.0 / np.sqrt(HD), 0.0, -1.0],
                     dtype=np.float32)
    cconst_d = nc.inline_tensor(np.tile(cvals[None, :], (128, 1)), name="cconst")

    kscr = dint("kscr", [H, WT], BF16)
    vtscr = dint("vtscr", [WT, H], BF16)
    xs1 = dint("xs1", [SQ, H])
    xs2 = dint("xs2", [SQ, H])
    xs3 = dint("xs3", [SQ, H])
    ot_scr = dint("ot_scr", [SQ, H])
    oscr = dint("oscr", [H, SQ], BF16)
    y_scr = dint("y_scr", [128, 2 * 16 * 512])
    agg_in = nc.dram_tensor("agg_in", [256, INTER], BF16)
    agu_in = nc.dram_tensor("agu_in", [256, INTER], BF16)
    agd_in = nc.dram_tensor("agd_in", [688, H], BF16)
    agg_out = nc.dram_tensor("agg_out", [H, INTER], BF16)
    agu_out = nc.dram_tensor("agu_out", [H, INTER], BF16)
    agd_out = nc.dram_tensor("agd_out", [INTER, H], BF16)
    cc_in = nc.dram_tensor("cc_in", [128, 512], F32)
    cc_out = nc.dram_tensor("cc_out", [1024, 512], F32)
    tl_in = nc.dram_tensor("tl_in", [128, 48], F32)
    tl_out = nc.dram_tensor("tl_out", [1024, 48], F32)
    dbg = {}
    if debug:
        dbg["xlnT"] = nc.dram_tensor("d_xlnT", [128, 16 * WT], BF16, kind="ExternalOutput")
        dbg["qbf"] = nc.dram_tensor("d_qbf", [128, 16 * SQ], BF16, kind="ExternalOutput")
        dbg["obf"] = nc.dram_tensor("d_obf", [128, 16 * SQ], BF16, kind="ExternalOutput")
        dbg["rstdk"] = nc.dram_tensor("d_rstdk", [128, WT], F32, kind="ExternalOutput")
        dbg["sv"] = nc.dram_tensor("d_sv", [128, 16 * SQ], BF16, kind="ExternalOutput")
        dbg["xconv"] = nc.dram_tensor("d_xconv", [128, 16 * SQ], BF16, kind="ExternalOutput")
        dbg["gfin"] = nc.dram_tensor("d_gfin", [128, 512], F32, kind="ExternalOutput")
        dbg["att_sc"] = nc.dram_tensor("d_att_sc", [128, 1152], F32, kind="ExternalOutput")
        dbg["att_p1"] = nc.dram_tensor("d_att_p1", [128, 1152], BF16, kind="ExternalOutput")
        dbg["att_p2"] = nc.dram_tensor("d_att_p2", [128, 1152], BF16, kind="ExternalOutput")
        dbg["att_rs"] = nc.dram_tensor("d_att_rs", [128, 4], F32, kind="ExternalOutput")
        dbg["att_pt"] = nc.dram_tensor("d_att_pt", [128, 16384], BF16, kind="ExternalOutput")
        dbg["att_kr"] = nc.dram_tensor("d_att_kr", [128, 2048], BF16, kind="ExternalOutput")
        dbg["wg"] = nc.dram_tensor("d_wg", [128, 2 * INTER], BF16, kind="ExternalOutput")
        dbg["hact"] = nc.dram_tensor("d_hact", [128, 44032], BF16, kind="ExternalOutput")
        dbg["ot1"] = nc.dram_tensor("d_ot1", [SQ, H], F32, kind="ExternalOutput")
        dbg["ot2"] = nc.dram_tensor("d_ot2", [SQ, H], F32, kind="ExternalOutput")
        dbg["yc"] = nc.dram_tensor("d_yc", [128, 8192], F32, kind="ExternalOutput")

    b = Bld(nc, es)
    for s in ["d_sync", "d_scalar", "d_gpsimd", "d_vector", "d_tensor",
              "pe", "act", "dve", "pool", "cc", "bar"]:
        b.sem(s)

    def sbuf(name, shape, dt=F32):
        return es.enter_context(nc.sbuf_tensor(name, shape, dt))

    arena_bf_t = sbuf("arena_bf", [128, 61440], BF16)
    ABF = Arena(arena_bf_t, 61440)

    sb_consts = sbuf("c_consts", [128, 8])
    C_EPSLN, C_EPSRMS, C_ONE, C_NEGH = (sb_consts[:, i:i + 1] for i in range(4))
    C_IH, C_ISQH, C_ZERO, C_NEG1 = (sb_consts[:, i:i + 1] for i in range(4, 8))
    sb_gg = sbuf("c_gg", [128, 16])
    sb_gk = sbuf("c_gk", [128, 16])
    sb_bv = sbuf("c_bv", [128, 16])
    sb_bxc = sbuf("c_bxc", [128, 16])
    sb_aarr = sbuf("c_aarr", [128, 256])
    sb_carr = sbuf("c_carr", [128, 256])
    sb_conv = sbuf("c_conv", [128, 64])
    sb_lut = sbuf("c_lut", [128, 16])
    sb_l0d = sbuf("c_l0d", [128, 8])
    sb_l0l = sbuf("c_l0l", [128, 8])
    sb_idb = sbuf("c_idb", [128, 128], BF16)
    sb_idf = sbuf("c_idf", [128, 128])
    sb_ones = sbuf("c_ones", [128, 1])
    sb_onesr = sbuf("c_onesr", [1, 128])
    sb_onesb = sbuf("c_onesb", [128, 1], BF16)
    sb_ohbc = sbuf("c_ohbc", [16, 16 * 128])
    sb_bbt = sbuf("c_bbt", [16, 1])
    sb_bprm = sbuf("c_bprm", [16, 1])
    sb_bdl = sbuf("c_bdl", [1, 1])
    sb_sela1 = sbuf("c_sela1", [128, 8])
    sb_selap = sbuf("c_selap", [128, 8])
    sb_selb = sbuf("c_selb", [128, 8])
    sb_seltl = sbuf("c_seltl", [128, 8])
    sb_lrow = sbuf("c_lrow", [1, 16])

    sb_rstdq = sbuf("s_rstdq", [128, 8])
    sb_cexp = sbuf("s_cexp", [128, 8])
    sb_mx = sbuf("s_mx", [128, 1])
    sb_nb = sbuf("s_nb", [128, 1])
    sb_rsum = sbuf("s_rsum", [128, 1])
    sb_rcp = sbuf("s_rcp", [128, 1])
    sb_ln = sbuf("s_ln", [128, 8])
    sb_bn6 = sbuf("s_bn6", [128, 24])
    sb_pbf = sbuf("s_pbf", [128, 1152], BF16)
    sb_xst = sbuf("s_xst", [128, 2048])
    sb_xst2 = sbuf("s_xst2", [128, 2048])
    sb_xbf = sbuf("s_xbf", [128, 2048], BF16)
    sb_ev = sbuf("s_ev", [128, 1024])
    sb_ev2 = sbuf("s_ev2", [128, 1024])
    sb_wst = sbuf("s_wst", [128, 16, 128], BF16)
    sb_wst2 = sbuf("s_wst2", [128, 16, 128], BF16)
    sb_sqacc = sbuf("s_sqacc", [128, 2048])
    sb_qsqa = sbuf("s_qsqa", [128, 1024])
    sb_sqb = sbuf("s_sqb", [128, 512], BF16)
    sb_rows = sbuf("s_rows", [128, 1024])
    sb_stot = sbuf("s_stot", [128, 1])
    sb_carry = sbuf("s_carry", [128, 256])
    sb_gfin = sbuf("s_gfin", [128, 256])
    sb_gin = sbuf("s_gin", [128, 256])
    sb_acc1 = sbuf("s_acc1", [128, 256])
    sb_acc2 = sbuf("s_acc2", [128, 256])
    sb_tails = sbuf("s_tails", [128, 8, 48])
    sb_tprev = sbuf("s_tprev", [128, 48])

    G_SUB = 688

    ps_sc = es.enter_context(nc.psum_tensor("ps_sc", [128, 1152], F32))
    ps_pt = es.enter_context(nc.psum_tensor("ps_pt", [128, 2048], BF16))
    ps_o = es.enter_context(nc.psum_tensor("ps_o", [128, 1024], F32))
    ps_x = es.enter_context(nc.psum_tensor("ps_x", [128, 512], F32))

    for dst, src in [(sb_consts, cconst_d), (sb_bv, bias_v), (sb_bxc, bias_xc),
                     (sb_aarr, aarr_in), (sb_carr, carr_in), (sb_conv, convw_in),
                     (sb_idb, id_bf_d), (sb_idf, id_f_d), (sb_ones, ones_d),
                     (sb_sela1, sel_a1), (sb_selap, sel_ap), (sb_selb, sel_b),
                     (sb_seltl, sel_tl), (sb_gg, gq_in), (sb_gk, gk_in)]:
        b.dma("sync", dst[:, :], src[:, :])
    b.dma("sync", sb_onesr[0:1, :], onesr_d[:, :])
    b.dma("sync", sb_ohbc[0:16, :], ohbc_d[:, :])
    b.dma("sync", sb_bbt[0:16, :], bias_bt[:, :])
    b.dma("sync", sb_bprm[0:16, :], bparam_in[:, :])
    b.dma("sync", sb_bdl[0:1, :], bias_dl[:, :])
    b.dma("sync", sb_lrow[0:1, :], lutv[:, :])
    c0 = b.cnt["d_sync"]
    b.wait("vector", "d_sync", c0)
    b.op("vector", lambda v: v.tensor_copy(out=sb_onesb[:], in_=sb_ones[:]), inc="dve")
    b.wait("scalar", "d_sync", c0)
    b.op("scalar", lambda s: s.activation(out=sb_aarr[:], in_=sb_aarr[:], func=AF.Exp,
                                          scale=C_ONE, bias=C_ZERO), inc="act")
    b.wait("vector", "act", b.cnt["act"])
    b.op("vector", lambda v: v.tensor_scalar(out=sb_aarr[:], in0=sb_aarr[:], scalar1=-1.0,
                                             scalar2=None, op0=OP.mult), inc="dve")
    b.wait("tensor", "d_sync", c0)
    b.op("tensor", lambda t: t.matmul(ps_x[:, 0:16], lhsT=sb_onesr[0:1, :], rhs=sb_lrow[0:1, :],
                                      start=True, stop=True), inc="pe")
    b.wait("vector", "pe", b.cnt["pe"])
    b.op("vector", lambda v: v.tensor_copy(out=sb_lut[:], in_=ps_x[:, 0:16]), inc="dve")
    b.sw("vector", "dve")
    lut_e = sb_lut.ap().rearrange("p (a b) -> p a b", b=2)[:, :, 0:1].rearrange("p a b -> p (a b)")
    lut_o = sb_lut.ap().rearrange("p (a b) -> p a b", b=2)[:, :, 1:2].rearrange("p a b -> p (a b)")
    b.op("vector", lambda v: v.tensor_tensor(out=sb_l0d[:], in0=lut_o, in1=lut_e, op=OP.subtract), inc="dve")
    b.op("vector", lambda v: v.tensor_copy(out=sb_l0l[:], in_=lut_e), inc="dve")
    b.sw("vector", "dve")
    b.barrier()

    ABF.push()
    g_land = ABF.al(2752, "g_land")
    g_f1 = ABF.al(G_SUB, "g_f1")
    g_t8 = ABF.al3(8, G_SUB, "g_t8")
    g_m1 = ABF.ali8(G_SUB, "g_m1")
    g_m2 = ABF.ali8(G_SUB, "g_m2")
    g_m3 = ABF.ali8(G_SUB, "g_m3")

    def ghost_convert(land_ap, agin, r0, pr, c0, ncol_h, sub):
        for s0 in range(ncol_h // sub):
            lo, w = s0 * sub, sub
            f0, f1 = land_ap[:, lo:lo + w], g_f1[0:pr, 0:w]
            m1, m2, m3 = g_m1[0:pr, 0:w], g_m2[0:pr, 0:w], g_m3[0:pr, 0:w]
            t8 = g_t8[0:pr, :, 0:w] if pr < 128 else g_t8[:, :, 0:w]
            b.op("vector", lambda v, f0=f0, m3=m3: v.tensor_scalar(out=m3, in0=f0, scalar1=7.5, scalar2=None, op0=OP.is_ge), inc="dve")
            b.sw("vector", "dve")
            b.op("vector", lambda v, m3=m3, f0=f0, f1=f1: v.scalar_tensor_tensor(out=f1, in0=m3, scalar=-8.0, in1=f0, op0=OP.mult, op1=OP.add), inc="dve")
            b.sw("vector", "dve")
            b.op("vector", lambda v, f1=f1, m2=m2: v.tensor_scalar(out=m2, in0=f1, scalar1=3.5, scalar2=None, op0=OP.is_ge), inc="dve")
            b.sw("vector", "dve")
            b.op("vector", lambda v, m2=m2, f1=f1, f0=f0, t8=t8: v.scalar_tensor_tensor(out=t8[:, 7, :], in0=m2, scalar=-4.0, in1=f1, op0=OP.mult, op1=OP.add), inc="dve")
            b.sw("vector", "dve")
            b.op("vector", lambda v, f0=f0, m1=m1, t8=t8: v.tensor_scalar(out=m1, in0=t8[:, 7, :], scalar1=1.5, scalar2=None, op0=OP.is_ge), inc="dve")
            b.sw("vector", "dve")
            b.op("vector", lambda v, m1=m1, f1=f1, t8=t8: v.scalar_tensor_tensor(out=f1, in0=m1, scalar=-2.0, in1=t8[:, 7, :], op0=OP.mult, op1=OP.add), inc="dve")
            b.sw("vector", "dve")
            for j in range(8):
                b.op("vector", lambda v, f1=f1, j=j, t8=t8, pr=pr: v.tensor_scalar(
                    out=t8[:, j, :], in0=f1, scalar1=sb_l0d[0:pr, j:j + 1],
                    scalar2=sb_l0l[0:pr, j:j + 1], op0=OP.mult, op1=OP.add), inc="dve")
            b.sw("vector", "dve")
            for j in range(4):
                b.op("vector", lambda v, j=j, m1=m1, t8=t8: v.copy_predicated(
                    out=t8[:, 2 * j, :], mask=m1, data=t8[:, 2 * j + 1, :]), inc="dve")
            b.sw("vector", "dve")
            for j in range(2):
                b.op("vector", lambda v, j=j, m2=m2, t8=t8: v.copy_predicated(
                    out=t8[:, 4 * j, :], mask=m2, data=t8[:, 4 * j + 2, :]), inc="dve")
            b.sw("vector", "dve")
            b.op("vector", lambda v, m3=m3, t8=t8: v.copy_predicated(
                out=t8[:, 0, :], mask=m3, data=t8[:, 4, :]), inc="dve")
            b.sw("vector", "dve")
            b.wait("sync", "dve", b.cnt["dve"])
            b.dma("sync", agin[r0:r0 + pr, c0 + lo:c0 + lo + w], t8[:, 0, :])
            b.wait("vector", "d_sync", b.cnt["d_sync"])

    for idxt, agin, agout, nrows, ncols, sub in [
            (gidxT, agg_in, agg_out, 256, INTER, 688),
            (uidxT, agu_in, agu_out, 256, INTER, 688),
            (didxT, agd_in, agd_out, 688, H, 512)]:
        ncol_h = ncols // 2
        for ch in range((nrows + 127) // 128):
            r0 = ch * 128
            pr = min(128, nrows - r0)
            for hf in range(2):
                c0 = hf * ncol_h
                b.wait("gpsimd", "dve", b.cnt["dve"])
                dl = b.dma("gpsimd", g_land[0:pr, 0:ncol_h], idxt[r0:r0 + pr, c0:c0 + ncol_h])
                b.wait("vector", "d_gpsimd", dl)
                ghost_convert(g_land[0:pr, :] if pr < 128 else g_land, agin, r0, pr, c0, ncol_h, sub)
        b.wait("gpsimd", "d_sync", b.cnt["d_sync"])
        if ENABLE_CC:
            b.op("gpsimd", lambda g, agin=agin, agout=agout: g.collective_compute(
                "AllGather", OP.bypass, replica_groups=[list(range(8))],
                ins=[agin.ap().opt()], outs=[agout.ap().opt()]), inc="cc", inc_by=1)
        else:
            b.op("gpsimd", lambda g: g.nop(), inc="cc", inc_by=1)
    cc_ffn = b.cnt["cc"]
    ABF.pop()
    b.barrier()

    def emit_ln(src_a, src_b, dst_x, lnT, wt0, ntiles):
        for tb in range(ntiles):
            r0 = tb * 128
            da = b.dma("sync", sb_xst[:, :], src_a[r0:r0 + 128, :])
            if src_b is not None:
                db = b.dma("sync", sb_xst2[:, :], src_b[r0:r0 + 128, :])
                b.wait("vector", "d_sync", db)
                b.op("vector", lambda v: v.tensor_tensor(out=sb_xst[:, :], in0=sb_xst[:, :],
                                                         in1=sb_xst2[:, :], op=OP.add), inc="dve")
                if dst_x is not None:
                    b.wait("sync", "dve", b.cnt["dve"])
                    b.dma("sync", dst_x[r0:r0 + 128, :], sb_xst[:, :])
            else:
                b.wait("vector", "d_sync", da)
            b.op("vector", lambda v: v.reduce_sum(out=sb_ln[:, 0:1], in_=sb_xst[:, :],
                                                  axis=AX.X), inc="dve")
            b.wait("scalar", "dve", b.cnt["dve"])
            b.op("scalar", lambda s: s.activation(out=sb_xst2[:, :], in_=sb_xst[:, :],
                                                  func=AF.Square, scale=C_ONE, bias=C_ZERO), inc="act")
            b.wait("vector", "act", b.cnt["act"])
            b.op("vector", lambda v: v.reduce_sum(out=sb_ln[:, 1:2], in_=sb_xst2[:, :],
                                                  axis=AX.X), inc="dve")
            b.sw("vector", "dve")
            b.op("vector", lambda v: v.tensor_scalar(out=sb_ln[:, 4:5], in0=sb_ln[:, 0:1],
                                                     scalar1=-1.0 / H, scalar2=None, op0=OP.mult), inc="dve")
            b.sw("vector", "dve")
            b.op("vector", lambda v: v.tensor_scalar(out=sb_ln[:, 5:6], in0=sb_ln[:, 1:2],
                                                     scalar1=1.0 / H, scalar2=None, op0=OP.mult), inc="dve")
            b.sw("vector", "dve")
            b.op("vector", lambda v: v.tensor_tensor(out=sb_ln[:, 6:7], in0=sb_ln[:, 4:5],
                                                     in1=sb_ln[:, 4:5], op=OP.mult), inc="dve")
            b.sw("vector", "dve")
            b.op("vector", lambda v: v.tensor_tensor(out=sb_ln[:, 1:2], in0=sb_ln[:, 5:6],
                                                     in1=sb_ln[:, 6:7], op=OP.subtract), inc="dve")
            b.sw("vector", "dve")
            b.wait("scalar", "dve", b.cnt["dve"])
            b.op("scalar", lambda s: s.activation(out=sb_ln[:, 2:3], in_=sb_ln[:, 1:2],
                                                  func=AF.Ln, scale=C_ONE, bias=C_EPSLN), inc="act")
            b.sw("scalar", "act")
            b.op("scalar", lambda s: s.activation(out=sb_ln[:, 3:4], in_=sb_ln[:, 2:3],
                                                  func=AF.Exp, scale=C_NEGH, bias=C_ZERO), inc="act")
            b.wait("vector", "act", b.cnt["act"])
            b.op("vector", lambda v: v.tensor_scalar(
                out=sb_xbf[:, :], in0=sb_xst[:, :], scalar1=sb_ln[:, 4:5],
                scalar2=sb_ln[:, 3:4], op0=OP.add, op1=OP.mult), inc="dve")
            b.wait("tensor", "dve", b.cnt["dve"])
            for c in range(16):
                b.op("tensor", lambda t, c=c: t.transpose(
                    ps_pt[:, c * 128:(c + 1) * 128], in_=sb_xbf[:, c * 128:(c + 1) * 128],
                    identity=sb_idb[:, :]), inc=("pe" if c == 15 else None))
            b.wait("scalar", "pe", b.cnt["pe"])
            b.op("scalar", lambda s, wt0=wt0, tb=tb: s.copy(
                out=lnT[:, :, wt0 + tb * 128:wt0 + tb * 128 + 128],
                in_=ps_pt.ap().rearrange("p (c t) -> p c t", t=128)), inc="act")
            b.wait("tensor", "act", b.cnt["act"])
            b.wait("sync", "dve", b.cnt["dve"])
            b.wait("sync", "act", b.cnt["act"])
            b.wait("vector", "pe", b.cnt["pe"])

    ABF.push()
    qbf = ABF.al3(16, SQ, "qbf")
    trigC = ABF.alf(WT, "trigC")
    trigS = ABF.alf(WT, "trigS")
    ABF.push()
    xlnT = ABF.al3(16, WT, "xlnT")
    sb_kstage = ABF.al(WT, "kstage")
    sb_vstage = ABF.al3(16, 128, "vstage")
    emit_ln(x_halo, None, None, xlnT, 0, 8)
    emit_ln(x_own, None, None, xlnT, 1024, 8)
    if debug:
        b.wait("sync", "act", b.cnt["act"])
        b.dma("sync", dbg["xlnT"][:, :], xlnT.rearrange("p c t -> p (c t)"))
        b.wait("sync", "d_sync", b.cnt["d_sync"])
    b.barrier()

    b.dma("sync", trigC[:, :], cosw_in[:, :])
    b.dma("sync", trigS[:, :], sinw_in[:, :])
    cw0 = b.cnt["d_sync"]
    b.wait("vector", "d_sync", cw0)
    b.op("vector", lambda v: v.memset(sb_qsqa[:, :], 0.0), inc="dve")
    b.op("vector", lambda v: v.memset(sb_sqacc[:, :], 0.0), inc="dve")

    def rope_full(xin, w0, w1, out, swap_scr, t_scr, u_scr):
        W = w1 - w0
        b.op("vector", lambda v: v.tensor_copy(out=swap_scr[0:64, 0:W], in_=xin[64:128, :]), inc="dve")
        b.op("vector", lambda v: v.tensor_copy(out=swap_scr[64:128, 0:W], in_=xin[0:64, :]), inc="dve")
        b.sw("vector", "dve")
        b.op("vector", lambda v: v.tensor_tensor(out=t_scr[:, 0:W], in0=xin,
                                                 in1=trigC[:, w0:w1], op=OP.mult), inc="dve")
        b.op("vector", lambda v: v.tensor_tensor(out=u_scr[:, 0:W], in0=swap_scr[:, 0:W],
                                                 in1=trigS[:, w0:w1], op=OP.mult), inc="dve")
        b.sw("vector", "dve")
        b.op("vector", lambda v: v.tensor_tensor(out=out, in0=t_scr[:, 0:W],
                                                 in1=u_scr[:, 0:W], op=OP.add), inc="dve")

    wbufs = [sb_wst, sb_wst2]
    pe_mark = {}
    reg_free = {0: 0, 1: 0}
    for dc in range(48):
        buf = wbufs[dc % 2]
        if dc >= 2:
            b.wait("gpsimd", "pe", pe_mark[dc - 2])
        wsrc = wqkv.ap().rearrange("(kc p) f -> p kc f", p=128)[:, :, dc * 128:(dc + 1) * 128]
        wd = b.dma("gpsimd", buf[:, :, :], wsrc)
        b.wait("tensor", "d_gpsimd", wd)
        is_q, is_k = dc < 16, 16 <= dc < 32
        frs = [1024, 1536] if is_q else [0, 512, 1024, 1536]
        for fi, f0 in enumerate(frs):
            reg = ps_sc[:, (fi % 2) * 512:(fi % 2) * 512 + 512]
            b.wait("tensor", "act", reg_free[fi % 2])
            for kc in range(16):
                b.op("tensor", lambda t, reg=reg, buf=buf, kc=kc, f0=f0: t.matmul(
                    reg, lhsT=buf[:, kc, :], rhs=xlnT[:, kc, f0:f0 + 512],
                    start=(kc == 0), stop=(kc == 15)), inc=("pe" if kc == 15 else None))
            b.wait("scalar", "pe", b.cnt["pe"])
            b.wait("scalar", "dve", b.cnt["dve"])
            if is_q:
                b.op("scalar", lambda s, reg=reg: s.activation(
                    out=sb_sqb[:, :], in_=reg, func=AF.Square, scale=C_ONE, bias=C_ZERO), inc="act")
                b.op("scalar", lambda s, reg=reg, dc=dc: s.activation(
                    out=sb_ev[:, 0:512], in_=reg, func=AF.Copy, scale=sb_gg[:, dc:dc + 1]), inc="act")
                reg_free[fi % 2] = b.cnt["act"]
                b.wait("vector", "act", b.cnt["act"])
                fq = f0 - 1024
                b.op("vector", lambda v, fq=fq: v.tensor_tensor(
                    out=sb_qsqa[:, fq:fq + 512], in0=sb_qsqa[:, fq:fq + 512],
                    in1=sb_sqb[:, :], op=OP.add), inc="dve")
                rope_full(sb_ev[:, 0:512], 1024 + fq, 1024 + fq + 512,
                          qbf[:, dc, fq:fq + 512], sb_ev2[:, 0:512],
                          sb_ev2[:, 512:1024], sb_ev2[:, 0:512])
            elif is_k:
                b.op("scalar", lambda s, reg=reg: s.activation(
                    out=sb_sqb[:, :], in_=reg, func=AF.Square, scale=C_ONE, bias=C_ZERO), inc="act")
                b.op("scalar", lambda s, reg=reg, f0=f0, dc=dc: s.activation(
                    out=sb_kstage[:, f0:f0 + 512], in_=reg, func=AF.Copy,
                    scale=sb_gk[:, dc - 16:dc - 15]), inc="act")
                reg_free[fi % 2] = b.cnt["act"]
                b.wait("vector", "act", b.cnt["act"])
                b.op("vector", lambda v, f0=f0: v.tensor_tensor(
                    out=sb_sqacc[:, f0:f0 + 512], in0=sb_sqacc[:, f0:f0 + 512],
                    in1=sb_sqb[:, :], op=OP.add), inc="dve")
            else:
                b.op("scalar", lambda s, reg=reg, f0=f0: s.activation(
                    out=sb_kstage[:, f0:f0 + 512], in_=reg, func=AF.Copy, scale=C_ONE), inc="act")
                reg_free[fi % 2] = b.cnt["act"]
        if is_k:
            b.wait("sync", "act", b.cnt["act"])
            b.dma("sync", kscr[(dc - 16) * 128:(dc - 15) * 128, :], sb_kstage[:, :])
            b.wait("scalar", "d_sync", b.cnt["d_sync"])
        elif not is_q:
            b.wait("tensor", "act", b.cnt["act"])
            for tb in range(16):
                b.op("tensor", lambda t, tb=tb: t.transpose(
                    ps_pt[:, tb * 128:(tb + 1) * 128], in_=sb_kstage[:, tb * 128:(tb + 1) * 128],
                    identity=sb_idb[:, :]), inc=("pe" if tb == 15 else None))
            b.wait("scalar", "pe", b.cnt["pe"])
            b.op("scalar", lambda s: s.copy(out=sb_vstage.rearrange("p a c -> p (a c)"),
                                            in_=ps_pt[:, :]), inc="act")
            b.wait("sync", "act", b.cnt["act"])
            vc = dc - 32
            vdst = vtscr.ap().rearrange("(tb p) (c f) -> p tb c f", p=128, f=128)[:, :, vc, :]
            b.dma("sync", vdst, sb_vstage)
            b.wait("scalar", "d_sync", b.cnt["d_sync"])
            b.wait("tensor", "act", b.cnt["act"])
        pe_mark[dc] = b.cnt["pe"]

    b.wait("tensor", "dve", b.cnt["dve"])
    for fi in range(2):
        b.op("tensor", lambda t, fi=fi: t.matmul(
            ps_x[0:1, 0:512], lhsT=sb_ones[:, :], rhs=sb_qsqa[:, fi * 512:fi * 512 + 512],
            start=True, stop=True), inc="pe")
        b.wait("vector", "pe", b.cnt["pe"])
        b.op("vector", lambda v, fi=fi: v.tensor_copy(out=sb_ev[0:1, fi * 512:fi * 512 + 512],
                                                      in_=ps_x[0:1, 0:512]), inc="dve")
        b.wait("tensor", "dve", b.cnt["dve"])
    for fi in range(4):
        b.op("tensor", lambda t, fi=fi: t.matmul(
            ps_x[0:1, 0:512], lhsT=sb_ones[:, :], rhs=sb_sqacc[:, fi * 512:fi * 512 + 512],
            start=True, stop=True), inc="pe")
        b.wait("vector", "pe", b.cnt["pe"])
        b.op("vector", lambda v, fi=fi: v.tensor_copy(
            out=sb_xst[0:1, fi * 512:fi * 512 + 512], in_=ps_x[0:1, 0:512]), inc="dve")
        b.wait("tensor", "dve", b.cnt["dve"])
    b.wait("scalar", "dve", b.cnt["dve"])
    b.op("scalar", lambda s: s.activation(out=sb_ev[0:1, :], in_=sb_ev[0:1, :], func=AF.Ln,
                                          scale=C_IH[0:1, :], bias=C_EPSRMS[0:1, :]), inc="act")
    b.sw("scalar", "act")
    b.op("scalar", lambda s: s.activation(out=sb_ev[0:1, :], in_=sb_ev[0:1, :], func=AF.Exp,
                                          scale=C_NEGH[0:1, :], bias=C_ZERO[0:1, :]), inc="act")
    b.sw("scalar", "act")
    b.op("scalar", lambda s: s.activation(out=sb_xst[0:1, :], in_=sb_xst[0:1, :], func=AF.Ln,
                                          scale=C_IH[0:1, :], bias=C_EPSRMS[0:1, :]), inc="act")
    b.sw("scalar", "act")
    b.op("scalar", lambda s: s.activation(out=sb_xst[0:1, :], in_=sb_xst[0:1, :], func=AF.Exp,
                                          scale=C_NEGH[0:1, :], bias=C_ZERO[0:1, :]), inc="act")
    b.sw("scalar", "act")
    b.wait("tensor", "act", b.cnt["act"])
    for fi in range(4):
        b.op("tensor", lambda t, fi=fi: t.matmul(
            ps_x[:, 0:512], lhsT=sb_onesr[0:1, :], rhs=sb_xst[0:1, fi * 512:fi * 512 + 512],
            start=True, stop=True), inc="pe")
        b.wait("vector", "pe", b.cnt["pe"])
        b.op("vector", lambda v, fi=fi: v.tensor_copy(out=sb_xst[:, fi * 512:fi * 512 + 512],
                                                      in_=ps_x[:, 0:512]), inc="dve")
        b.wait("tensor", "dve", b.cnt["dve"])
    b.op("vector", lambda v: v.tensor_tensor(out=trigC[:, :], in0=trigC[:, :],
                                             in1=sb_xst[:, :], op=OP.mult), inc="dve")
    b.op("vector", lambda v: v.tensor_tensor(out=trigS[:, :], in0=trigS[:, :],
                                             in1=sb_xst[:, :], op=OP.mult), inc="dve")
    if debug:
        b.wait("sync", "dve", b.cnt["dve"])
        b.dma("sync", dbg["rstdk"][:, :], sb_xst[:, :])
        b.dma("sync", dbg["qbf"][:, :], qbf.rearrange("p c t -> p (c t)"))
        b.wait("sync", "d_sync", b.cnt["d_sync"])
    b.wait("tensor", "dve", b.cnt["dve"])
    for fi in range(2):
        b.op("tensor", lambda t, fi=fi: t.matmul(
            ps_x[:, 0:512], lhsT=sb_onesr[0:1, :], rhs=sb_ev[0:1, fi * 512:fi * 512 + 512],
            start=True, stop=True), inc="pe")
        b.wait("vector", "pe", b.cnt["pe"])
        b.op("vector", lambda v, fi=fi: v.tensor_copy(out=sb_sqacc[:, fi * 512:fi * 512 + 512],
                                                      in_=ps_x[:, 0:512]), inc="dve")
        b.wait("tensor", "dve", b.cnt["dve"])
    for qb in range(8):
        b.op("tensor", lambda t, qb=qb: t.transpose(
            ps_x[:, 0:128], in_=sb_sqacc[:, qb * 128:qb * 128 + 128],
            identity=sb_idf[:, :]), inc="pe")
        b.wait("vector", "pe", b.cnt["pe"])
        b.op("vector", lambda v, qb=qb: v.tensor_copy(out=sb_rstdq[:, qb:qb + 1],
                                                      in_=ps_x[:, 0:1]), inc="dve")
        b.wait("tensor", "dve", b.cnt["dve"])
    b.sw("vector", "dve")
    b.op("vector", lambda v: v.tensor_scalar(out=sb_cexp[:, :], in0=sb_rstdq[:, :],
                                             scalar1=C_ISQH, scalar2=None, op0=OP.mult), inc="dve")
    b.sw("vector", "dve")
    ABF.pop()
    b.barrier()

    ABF.push()
    ptbuf = ABF.al3(16, SQ, "ptbuf")
    kraw = [ABF.al(WT, "kraw0"), ABF.al(WT, "kraw1")]
    krope = [ABF.al(WT, "krope0"), ABF.al(WT, "krope1")]
    vth = [ABF.al3(16, 128, "vth0"), ABF.al3(16, 128, "vth1")]
    ostage = ABF.al(SQ, "ostage")
    sb_maski = ABF.ali8(8 * 1152, "maski").rearrange("p (q k) -> p q k", k=1152)
    mld = b.dma("sync", sb_maski, maskmul.ap().rearrange("q p k -> p q k"))
    b.wait("vector", "d_sync", mld)
    pv_done = {}
    score_done = {}
    rope_mark = {}
    for h in range(16):
        kb, kr, vb = kraw[h % 2], krope[h % 2], vth[h % 2]
        if h >= 2:
            b.wait("sync", "pe", pv_done[h - 2])
            b.wait("sync", "dve", rope_mark[h - 2])
        kld = b.dma("sync", kb, kscr[h * 128:(h + 1) * 128, :])
        vld = b.dma("sync", vb, vtscr.ap().rearrange("(tb p) (c f) -> p tb c f", p=128, f=128)[:, :, h, :])
        b.wait("vector", "d_sync", kld)
        if h >= 2:
            b.wait("vector", "pe", score_done[h - 2])
        for hh in range(2):
            sl = slice(hh * 1024, hh * 1024 + 1024)
            rope_full(kb[:, sl], hh * 1024, hh * 1024 + 1024, kr[:, sl],
                      sb_xbf[:, 0:1024], sb_ev, sb_ev2)
        rope_dve = b.cnt["dve"]
        rope_mark[h] = rope_dve
        b.wait("tensor", "dve", rope_dve)
        b.wait("tensor", "d_sync", vld)
        for qb in range(8):
            for j, (f0, fw) in enumerate([(0, 512), (512, 512), (1024, 128)]):
                b.op("tensor", lambda t, qb=qb, f0=f0, fw=fw, h=h, kr=kr: t.matmul(
                    ps_sc[:, f0:f0 + fw], lhsT=qbf[:, h, qb * 128:(qb + 1) * 128],
                    rhs=kr[:, qb * 128 + f0:qb * 128 + f0 + fw],
                    start=True, stop=True), inc=("pe" if j == 2 else None))
            b.wait("vector", "pe", b.cnt["pe"])
            b.op("vector", lambda v: v.reduce_max(out=sb_mx[:, 0:1], in_=ps_sc[:, :], axis=AX.X), inc="dve")
            b.sw("vector", "dve")
            b.op("vector", lambda v, qb=qb: v.tensor_scalar(
                out=sb_nb[:, 0:1], in0=sb_mx[:, 0:1], scalar1=sb_cexp[:, qb:qb + 1],
                scalar2=-1.0, op0=OP.mult, op1=OP.mult), inc="dve")
            b.wait("scalar", "dve", b.cnt["dve"])
            if debug and h == 0 and qb == 3:
                b.op("scalar", lambda s: s.copy(out=sb_ev[:, :], in_=ps_sc[:, 0:1024]), inc="act")
                b.op("scalar", lambda s: s.copy(out=sb_ev2[:, 0:128], in_=ps_sc[:, 1024:1152]), inc="act")
                b.wait("sync", "act", b.cnt["act"])
                b.dma("sync", dbg["att_sc"][:, 0:1024], sb_ev[:, :])
                b.dma("sync", dbg["att_sc"][:, 1024:1152], sb_ev2[:, 0:128])
                b.wait("scalar", "d_sync", b.cnt["d_sync"])
            b.op("scalar", lambda s, qb=qb: s.activation(
                out=sb_pbf[:, :], in_=ps_sc[:, :], func=AF.Exp,
                scale=sb_cexp[:, qb:qb + 1], bias=sb_nb[:, 0:1]), inc="act")
            b.wait("vector", "act", b.cnt["act"])
            if debug and h == 0 and qb == 3:
                b.wait("sync", "act", b.cnt["act"])
                b.dma("sync", dbg["att_p1"][:, :], sb_pbf[:, :])
                b.wait("vector", "d_sync", b.cnt["d_sync"])
            b.op("vector", lambda v, qb=qb: v.scalar_tensor_tensor(
                out=sb_pbf[:, :], in0=sb_pbf[:, :], scalar=1.0, in1=sb_maski[:, qb, :],
                op0=OP.mult, op1=OP.mult, accum_out=sb_rsum[:, 0:1]), inc="dve")
            b.sw("vector", "dve")
            b.op("vector", lambda v: v.reciprocal(out=sb_rcp[:, 0:1], in_=sb_rsum[:, 0:1]), inc="dve")
            b.sw("vector", "dve")
            b.op("vector", lambda v: v.tensor_scalar(out=sb_pbf[:, :], in0=sb_pbf[:, :],
                                                     scalar1=sb_rcp[:, 0:1], scalar2=None,
                                                     op0=OP.mult), inc="dve")
            if debug and h == 0 and qb == 3:
                b.op("vector", lambda v: v.tensor_copy(out=sb_ev[:, 0:4],
                    in_=sb_mx[:, 0:1]), inc="dve") if False else None
                b.op("vector", lambda v: v.tensor_copy(out=sb_ev[:, 0:1], in_=sb_mx[:, 0:1]), inc="dve")
                b.op("vector", lambda v: v.tensor_copy(out=sb_ev[:, 1:2], in_=sb_nb[:, 0:1]), inc="dve")
                b.op("vector", lambda v: v.tensor_copy(out=sb_ev[:, 2:3], in_=sb_rsum[:, 0:1]), inc="dve")
                b.op("vector", lambda v: v.tensor_copy(out=sb_ev[:, 3:4], in_=sb_rcp[:, 0:1]), inc="dve")
                b.wait("sync", "dve", b.cnt["dve"])
                b.dma("sync", dbg["att_p2"][:, :], sb_pbf[:, :])
                b.dma("sync", dbg["att_rs"][:, :], sb_ev[:, 0:4])
                b.dma("sync", dbg["att_kr"][:, :], kr)
                b.wait("vector", "d_sync", b.cnt["d_sync"])
            b.wait("tensor", "dve", b.cnt["dve"])
            for j in range(9):
                b.op("tensor", lambda t, j=j: t.transpose(
                    ps_pt[:, j * 128:(j + 1) * 128], in_=sb_pbf[:, j * 128:(j + 1) * 128],
                    identity=sb_idb[:, :]), inc=("pe" if j == 8 else None))
            b.wait("scalar", "pe", b.cnt["pe"])
            b.op("scalar", lambda s, qb=qb: s.copy(
                out=ptbuf[:, qb:qb + 9, qb * 128:qb * 128 + 128],
                in_=ps_pt.ap().rearrange("p (a t) -> p a t", t=128)[:, 0:9, :]), inc="act")
            b.wait("tensor", "act", b.cnt["act"])
        score_done[h] = b.cnt["pe"]
        if debug and h == 0:
            b.wait("sync", "act", b.cnt["act"])
            b.dma("sync", dbg["att_pt"][:, :], ptbuf.rearrange("p a t -> p (a t)"))
            b.wait("sync", "d_sync", b.cnt["d_sync"])
        for wb in range(16):
            jlo, jhi = max(0, wb - 8), min(7, wb)
            if wb <= 7:
                b.op("tensor", lambda t, wb=wb, vb=vb: t.matmul(
                    ps_o[:, wb * 128:wb * 128 + 128], lhsT=vb[:, wb, :],
                    rhs=ptbuf[:, wb, wb * 128:wb * 128 + 128], start=True, stop=True), inc="pe")
            lo, hi = jlo * 128, (jhi + (1 if wb > 7 else 0)) * 128
            f = lo
            while f < hi:
                fw = min((f // 512 + 1) * 512, hi) - f
                b.op("tensor", lambda t, wb=wb, vb=vb, f=f, fw=fw: t.matmul(
                    ps_o[:, f:f + fw], lhsT=vb[:, wb, :], rhs=ptbuf[:, wb, f:f + fw],
                    start=False, stop=True), inc="pe")
                f += fw
        pv_done[h] = b.cnt["pe"]
        b.wait("scalar", "pe", pv_done[h])
        b.wait("scalar", "d_sync", b.cnt["d_sync"])
        b.op("scalar", lambda s, h=h: s.copy(out=ostage, in_=ps_o[:, :]), inc="act")
        b.wait("sync", "act", b.cnt["act"])
        b.dma("sync", oscr[h * 128:(h + 1) * 128, :], ostage)
        b.wait("tensor", "act", b.cnt["act"])
    b.barrier()

    def evac_transpose_to_ot(dc):
        b.wait("scalar", "pe", b.cnt["pe"])
        b.op("scalar", lambda s: s.copy(out=sb_ev[:, :], in_=ps_o[:, :]), inc="act")
        b.wait("tensor", "act", b.cnt["act"])
        for j in range(8):
            b.op("tensor", lambda t, j=j: t.transpose(
                ps_sc[:, j * 128:(j + 1) * 128], in_=sb_ev[:, j * 128:(j + 1) * 128],
                identity=sb_idf[:, :]), inc=("pe" if j == 7 else None))
        b.wait("vector", "pe", b.cnt["pe"])
        b.op("vector", lambda v: v.tensor_copy(out=sb_ev2[:, :], in_=ps_sc[:, 0:1024]), inc="dve")
        b.wait("sync", "dve", b.cnt["dve"])
        odst = ot_scr.ap().rearrange("(tb p) f -> p tb f", p=128)[:, :, dc * 128:(dc + 1) * 128]
        b.dma("sync", odst, sb_ev2.ap().rearrange("p (tb f) -> p tb f", f=128))
        b.wait("scalar", "d_sync", b.cnt["d_sync"])
        b.wait("tensor", "d_sync", b.cnt["d_sync"])

    ABF.pop()
    ABF.pop()
    obf = ABF.al3(16, SQ, "obf_wo")
    old = b.dma("sync", obf, oscr.ap().rearrange("(c p) t -> p c t", p=128))
    b.wait("tensor", "d_sync", old)
    pm = {}
    for i in range(16):
        buf = wbufs[i % 2]
        if i >= 2:
            b.wait("gpsimd", "pe", pm[i - 2])
        wsrc = wo.ap().rearrange("(kc p) f -> p kc f", p=128)[:, :, i * 128:(i + 1) * 128]
        wd = b.dma("gpsimd", buf[:, :, :], wsrc)
        b.wait("tensor", "d_gpsimd", wd)
        for fh in range(2):
            for kc in range(16):
                b.op("tensor", lambda t, buf=buf, kc=kc, fh=fh: t.matmul(
                    ps_o[:, fh * 512:fh * 512 + 512], lhsT=buf[:, kc, :],
                    rhs=obf[:, kc, fh * 512:fh * 512 + 512],
                    start=(kc == 0), stop=(kc == 15)), inc=("pe" if kc == 15 else None))
        pm[i] = b.cnt["pe"]
        evac_transpose_to_ot(i)
    ABF.off = 0
    b.barrier()

    ffn_call_no = [0]

    def emit_ffn(lnT, hact):
        b.wait("sync", "cc", cc_ffn)
        if debug and ffn_call_no[0] == 0:
            dl = b.dma("sync", sb_xbf[:, :], agg_out.ap().rearrange("(a p) o -> p a o", p=128)[:, 0, 0:2048])
            b.wait("sync", "d_sync", dl)
            dl = b.dma("sync", dbg["wg"][:, 0:2048], sb_xbf[:, :])
            b.wait("sync", "d_sync", dl)
            dl = b.dma("sync", sb_xbf[:, :], agd_out.ap().rearrange("(a p) o -> p a o", p=128)[:, 0, 0:2048])
            b.wait("sync", "d_sync", dl)
            dl = b.dma("sync", dbg["wg"][:, 2048:4096], sb_xbf[:, :])
            b.wait("sync", "d_sync", dl)
        pmf = {}
        for oc in range(43):
            wg = agg_out.ap().rearrange("(kc p) o -> p kc o", p=128)[:, :, oc * 128:(oc + 1) * 128]
            wu = agu_out.ap().rearrange("(kc p) o -> p kc o", p=128)[:, :, oc * 128:(oc + 1) * 128]
            if oc >= 1:
                b.wait("sync", "dve", pmf[oc - 1])
            b.dma("sync", sb_wst[:, :, :], wg)
            d2 = b.dma("sync", sb_wst2[:, :, :], wu)
            b.wait("tensor", "d_sync", d2)
            for fh in range(2):
                for kc in range(16):
                    b.op("tensor", lambda t, kc=kc, fh=fh, lnT=lnT: t.matmul(
                        ps_o[:, fh * 512:fh * 512 + 512], lhsT=sb_wst[:, kc, :],
                        rhs=lnT[:, kc, fh * 512:fh * 512 + 512],
                        start=(kc == 0), stop=(kc == 15)), inc=("pe" if kc == 15 else None))
            for fh in range(2):
                for kc in range(16):
                    b.op("tensor", lambda t, kc=kc, fh=fh, lnT=lnT: t.matmul(
                        ps_sc[:, fh * 512:fh * 512 + 512], lhsT=sb_wst2[:, kc, :],
                        rhs=lnT[:, kc, fh * 512:fh * 512 + 512],
                        start=(kc == 0), stop=(kc == 15)), inc=("pe" if kc == 15 else None))
            b.wait("scalar", "pe", b.cnt["pe"])
            b.op("scalar", lambda s: s.activation(out=sb_xbf[:, 0:1024], in_=ps_o[:, :],
                                                  func=AF.Silu, scale=C_ONE, bias=C_ZERO), inc="act")
            b.op("scalar", lambda s: s.copy(out=sb_xbf[:, 1024:2048], in_=ps_sc[:, 0:1024]), inc="act")
            b.wait("vector", "act", b.cnt["act"])
            b.op("vector", lambda v, oc=oc, hact=hact: v.tensor_tensor(
                out=hact[:, oc, :], in0=sb_xbf[:, 0:1024], in1=sb_xbf[:, 1024:2048],
                op=OP.mult), inc="dve")
            pmf[oc] = b.cnt["dve"]
            b.wait("tensor", "dve", pmf[oc])
            if debug and ffn_call_no[0] == 0:
                b.wait("sync", "dve", b.cnt["dve"])
                dl = b.dma("sync", dbg["hact"][:, oc * 1024:(oc + 1) * 1024], hact[:, oc, :])
                b.wait("sync", "d_sync", dl)
        groups = [(0, 16), (16, 32), (32, 43)]
        dpm = {}
        ffn_call_no[0] += 1
        b.wait("sync", "dve", pmf[42])
        for dc in range(16):
            for gi, (k0, k1) in enumerate(groups):
                buf = wbufs[(dc * 3 + gi) % 2]
                if dc * 3 + gi >= 2:
                    b.wait("sync", "pe", dpm[dc * 3 + gi - 2])
                wsrc = agd_out.ap().rearrange("(kc p) f -> p kc f", p=128)[:, k0:k1, dc * 128:(dc + 1) * 128]
                wd = b.dma("sync", buf[:, 0:k1 - k0, :], wsrc)
                b.wait("tensor", "d_sync", wd)
                for fh in range(2):
                    for kc in range(k0, k1):
                        b.op("tensor", lambda t, buf=buf, kc=kc, k0=k0, fh=fh, hact=hact: t.matmul(
                            ps_o[:, fh * 512:fh * 512 + 512], lhsT=buf[:, kc - k0, :],
                            rhs=hact[:, kc, fh * 512:fh * 512 + 512],
                            start=(kc == 0), stop=(kc == 42)), inc=("pe" if kc == k1 - 1 else None))
                dpm[dc * 3 + gi] = b.cnt["pe"]
            evac_transpose_to_ot(dc)

    ABF.push()
    x1lnT = ABF.al3(16, SQ, "x1lnT")
    hact1 = ABF.al3(43, SQ, "hact1")
    emit_ln(x_own, ot_scr, xs1, x1lnT, 0, 8)
    b.barrier()
    emit_ffn(x1lnT, hact1)
    if debug:
        for tb in range(8):
            dl = b.dma("sync", sb_xst[:, :], ot_scr[tb * 128:(tb + 1) * 128, :])
            b.wait("sync", "d_sync", dl)
            dl = b.dma("sync", dbg["ot1"][tb * 128:(tb + 1) * 128, :], sb_xst[:, :])
            b.wait("sync", "d_sync", dl)
    ABF.pop()
    b.barrier()

    ABF.push()
    svbf = ABF.al3(16, SQ, "svbf")
    xcwbf = ABF.al3(16, SQ, "xcwbf")
    cbounce = ABF.al(SQ, "cbounce")
    ABF.push()
    x2lnT = ABF.al3(16, SQ, "x2lnT")
    emit_ln(xs1, ot_scr, xs2, x2lnT, 0, 8)
    b.barrier()

    pm = {}
    for dc in range(32):
        buf = wbufs[dc % 2]
        if dc >= 2:
            b.wait("gpsimd", "pe", pm[dc - 2])
        wsrc = win.ap().rearrange("(kc p) f -> p kc f", p=128)[:, :, dc * 128:(dc + 1) * 128]
        wd = b.dma("gpsimd", buf[:, :, :], wsrc)
        b.wait("tensor", "d_gpsimd", wd)
        for fh in range(2):
            for kc in range(16):
                b.op("tensor", lambda t, buf=buf, kc=kc, fh=fh: t.matmul(
                    ps_o[:, fh * 512:fh * 512 + 512], lhsT=buf[:, kc, :],
                    rhs=x2lnT[:, kc, fh * 512:fh * 512 + 512],
                    start=(kc == 0), stop=(kc == 15)), inc=("pe" if kc == 15 else None))
        pm[dc] = b.cnt["pe"]
        b.wait("scalar", "pe", pm[dc])
        if dc < 16:
            b.op("scalar", lambda s, dc=dc: s.activation(
                out=svbf[:, dc, :], in_=ps_o[:, :], func=AF.Silu, scale=C_ONE,
                bias=sb_bv[:, dc:dc + 1]), inc="act")
        else:
            b.op("scalar", lambda s, dc=dc: s.activation(
                out=xcwbf[:, dc - 16, :], in_=ps_o[:, :], func=AF.Silu, scale=C_ONE,
                bias=sb_bxc[:, dc - 16:dc - 15]), inc="act")
        b.wait("tensor", "act", b.cnt["act"])
    b.wait("gpsimd", "pe", pm[31])
    d1 = b.dma("gpsimd", sb_wst[:, :, 0:16], win_bt.ap().rearrange("(kc p) f -> p kc f", p=128))
    d2 = b.dma("gpsimd", sb_wst2[:, :, 0:1], win_dl.ap().rearrange("(kc p) f -> p kc f", p=128), allow_slow_non_contiguous=True)
    b.wait("tensor", "d_gpsimd", d2)
    for fh in range(2):
        for kc in range(16):
            b.op("tensor", lambda t, kc=kc, fh=fh: t.matmul(
                ps_x[0:16, 0:512], lhsT=sb_wst[:, kc, 0:16],
                rhs=x2lnT[:, kc, fh * 512:fh * 512 + 512],
                start=(kc == 0), stop=(kc == 15)), inc=("pe" if kc == 15 else None))
        b.wait("vector", "pe", b.cnt["pe"])
        b.op("vector", lambda v, fh=fh: v.tensor_scalar(
            out=sb_rows[0:16, fh * 512:fh * 512 + 512], in0=ps_x[0:16, 0:512],
            scalar1=sb_bbt[0:16, 0:1], scalar2=sb_bprm[0:16, 0:1],
            op0=OP.add, op1=OP.add), inc="dve")
        b.wait("tensor", "dve", b.cnt["dve"])
    for fh in range(2):
        for kc in range(16):
            b.op("tensor", lambda t, kc=kc, fh=fh: t.matmul(
                ps_x[0:1, 0:512], lhsT=sb_wst2[:, kc, 0:1],
                rhs=x2lnT[:, kc, fh * 512:fh * 512 + 512],
                start=(kc == 0), stop=(kc == 15)), inc=("pe" if kc == 15 else None))
        b.wait("vector", "pe", b.cnt["pe"])
        b.op("vector", lambda v, fh=fh: v.tensor_scalar(
            out=sb_rows[32:33, fh * 512:fh * 512 + 512], in0=ps_x[0:1, 0:512],
            scalar1=sb_bdl[0:1, 0:1], scalar2=None, op0=OP.add), inc="dve")
        b.wait("tensor", "dve", b.cnt["dve"])
    ABF.pop()
    R0 = sb_rows[32:33, :]
    SC1 = sb_ev[32:33, 0:1024]
    SC2 = sb_ev2[32:33, 0:1024]
    b.wait("scalar", "dve", b.cnt["dve"])
    b.op("scalar", lambda s: s.activation(out=SC1, in_=R0, func=AF.Abs, scale=C_ONE[32:33, :], bias=C_ZERO[32:33, :]), inc="act")
    b.op("scalar", lambda s: s.activation(out=SC1, in_=SC1, func=AF.Exp, scale=C_NEG1[32:33, :], bias=C_ZERO[32:33, :]), inc="act")
    b.op("scalar", lambda s: s.activation(out=SC1, in_=SC1, func=AF.Ln, scale=C_ONE[32:33, :], bias=C_ONE[32:33, :]), inc="act")
    b.op("scalar", lambda s: s.activation(out=SC2, in_=R0, func=AF.Relu, scale=C_ONE[32:33, :], bias=C_ZERO[32:33, :]), inc="act")
    b.wait("vector", "act", b.cnt["act"])
    b.op("vector", lambda v: v.tensor_tensor(out=R0, in0=SC1, in1=SC2, op=OP.add), inc="dve")
    b.sw("vector", "dve")
    b.wait("scalar", "dve", b.cnt["dve"])
    b.op("scalar", lambda s: s.activation(out=SC1, in_=R0, func=AF.Exp, scale=C_NEG1[32:33, :], bias=C_ZERO[32:33, :]), inc="act")
    b.wait("vector", "act", b.cnt["act"])
    b.op("vector", lambda v: v.tensor_scalar(out=SC1, in0=SC1, scalar1=1.0, scalar2=None, op0=OP.add), inc="dve")
    b.sw("vector", "dve")
    b.op("vector", lambda v: v.reciprocal(out=SC2, in_=SC1), inc="dve")
    b.sw("vector", "dve")
    b.op("vector", lambda v: v.tensor_tensor(out=R0, in0=R0, in1=SC2, op=OP.mult), inc="dve")
    b.op("vector", lambda v: v.memset(sb_ev[32:33, 0:1024], 1.0), inc="dve")
    b.sw("vector", "dve")
    b.op("vector", lambda v: v.tensor_tensor_scan(out=sb_qsqa[32:33, :], data0=sb_ev[32:33, 0:1024],
                                                  data1=R0, initial=0.0, op0=OP.mult, op1=OP.add), inc="dve")
    b.sw("vector", "dve")
    b.op("vector", lambda v: v.tensor_copy(out=sb_qsqa[0:1, 0:1024], in_=sb_rows[32:33, :]), inc="dve")
    b.sw("vector", "dve")
    b.wait("tensor", "dve", b.cnt["dve"])
    for fh in range(2):
        b.op("tensor", lambda t, fh=fh: t.matmul(
            ps_x[:, 0:512], lhsT=sb_onesr[0:1, :], rhs=sb_qsqa[0:1, fh * 512:fh * 512 + 512],
            start=True, stop=True), inc="pe")
        b.wait("vector", "pe", b.cnt["pe"])
        b.op("vector", lambda v, fh=fh: v.tensor_copy(out=sb_ev[:, fh * 512:fh * 512 + 512],
                                                      in_=ps_x[:, 0:512]), inc="dve")
        b.wait("tensor", "dve", b.cnt["dve"])
    b.op("vector", lambda v: v.tensor_copy(out=sb_qsqa[0:1, 0:1024], in_=sb_qsqa[32:33, :]), inc="dve")
    b.sw("vector", "dve")
    b.wait("tensor", "dve", b.cnt["dve"])
    for fh in range(2):
        b.op("tensor", lambda t, fh=fh: t.matmul(
            ps_x[:, 0:512], lhsT=sb_onesr[0:1, :], rhs=sb_qsqa[0:1, fh * 512:fh * 512 + 512],
            start=True, stop=True), inc="pe")
        b.wait("vector", "pe", b.cnt["pe"])
        b.op("vector", lambda v, fh=fh: v.tensor_copy(out=sb_ev2[:, fh * 512:fh * 512 + 512],
                                                      in_=ps_x[:, 0:512]), inc="dve")
        b.wait("tensor", "dve", b.cnt["dve"])
    b.sw("vector", "dve")
    b.op("vector", lambda v: v.tensor_tensor(out=sb_sqacc[0:16, 0:1024], in0=sb_rows[0:16, :],
                                             in1=sb_ev[0:16, :], op=OP.mult), inc="dve")
    b.op("vector", lambda v: v.tensor_copy(out=sb_stot[:, 0:1], in_=sb_ev2[:, 1023:1024]), inc="dve")
    b.sw("vector", "dve")

    b.op("vector", lambda v: v.tensor_copy(
        out=sb_tprev.ap().rearrange("p (c t) -> p c t", t=3),
        in_=xcwbf[:, :, 1021:1024]), inc="dve")
    b.wait("sync", "dve", b.cnt["dve"])
    b.dma("sync", tl_in[:, :], sb_tprev[:, :])
    b.wait("gpsimd", "d_sync", b.cnt["d_sync"])
    if ENABLE_CC:
        b.op("gpsimd", lambda g: g.collective_compute(
            "AllGather", OP.bypass, replica_groups=[list(range(8))],
            ins=[tl_in.ap().opt()], outs=[tl_out.ap().opt()]), inc="cc", inc_by=1)
    else:
        b.op("gpsimd", lambda g: g.nop(), inc="cc", inc_by=1)
    b.wait("sync", "cc", b.cnt["cc"])
    tl_all = b.dma("sync", sb_tails[:, :, :], tl_out.ap().rearrange("(r p) c -> p r c", p=128))
    b.wait("vector", "d_sync", tl_all)
    b.op("vector", lambda v: v.memset(sb_tprev[:, :], 0.0), inc="dve")
    b.sw("vector", "dve")
    for r in range(8):
        b.op("vector", lambda v, r=r: v.scalar_tensor_tensor(
            out=sb_tprev[:, :], in0=sb_tails[:, r, :], scalar=sb_seltl[:, r:r + 1],
            in1=sb_tprev[:, :], op0=OP.mult, op1=OP.add), inc="dve")
        b.sw("vector", "dve")

    def xcv(dblk):
        return cbounce if dblk == 0 else xcwbf[:, dblk - 1, :]
    tl3v = sb_tprev.ap().rearrange("p (c t) -> p c t", t=3)
    for dblk in range(16):
        w0, w1, w2, w3 = (sb_conv[:, dblk * 4 + k:dblk * 4 + k + 1] for k in range(4))
        xc = xcwbf[:, dblk, :]
        out = xcv(dblk)
        b.op("vector", lambda v, xc=xc, w3=w3, out=out: v.tensor_scalar(
            out=out, in0=xc, scalar1=w3, scalar2=None, op0=OP.mult), inc="dve")
        b.sw("vector", "dve")
        b.op("vector", lambda v, xc=xc, w2=w2, out=out: v.scalar_tensor_tensor(
            out=out[:, 1:1024], in0=xc[:, 0:1023], scalar=w2, in1=out[:, 1:1024],
            op0=OP.mult, op1=OP.add), inc="dve")
        b.sw("vector", "dve")
        b.op("vector", lambda v, xc=xc, w1=w1, out=out: v.scalar_tensor_tensor(
            out=out[:, 2:1024], in0=xc[:, 0:1022], scalar=w1, in1=out[:, 2:1024],
            op0=OP.mult, op1=OP.add), inc="dve")
        b.sw("vector", "dve")
        b.op("vector", lambda v, xc=xc, w0=w0, out=out: v.scalar_tensor_tensor(
            out=out[:, 3:1024], in0=xc[:, 0:1021], scalar=w0, in1=out[:, 3:1024],
            op0=OP.mult, op1=OP.add), inc="dve")
        b.sw("vector", "dve")
        tl3 = tl3v[:, dblk, :]
        b.op("vector", lambda v, tl3=tl3, out=out, w0=w0: v.scalar_tensor_tensor(
            out=out[:, 0:3], in0=tl3, scalar=w0, in1=out[:, 0:3], op0=OP.mult, op1=OP.add), inc="dve")
        b.sw("vector", "dve")
        b.op("vector", lambda v, tl3=tl3, out=out, w1=w1: v.scalar_tensor_tensor(
            out=out[:, 0:2], in0=tl3[:, 1:3], scalar=w1, in1=out[:, 0:2], op0=OP.mult, op1=OP.add), inc="dve")
        b.sw("vector", "dve")
        b.op("vector", lambda v, tl3=tl3, out=out, w2=w2: v.scalar_tensor_tensor(
            out=out[:, 0:1], in0=tl3[:, 2:3], scalar=w2, in1=out[:, 0:1], op0=OP.mult, op1=OP.add), inc="dve")
        b.sw("vector", "dve")
    if debug:
        b.wait("sync", "dve", b.cnt["dve"])
        b.wait("sync", "act", b.cnt["act"])
        b.dma("sync", dbg["sv"][:, :], svbf.rearrange("p c t -> p (c t)"))
        for dd in range(16):
            b.dma("sync", dbg["xconv"][:, dd * SQ:(dd + 1) * SQ], xcv(dd))
        b.wait("sync", "d_sync", b.cnt["d_sync"])

    rbb = ABF.al3(16, 512, "rbb")
    ubf = ABF.al(512, "ubf")
    acum = ABF.al(512, "acum")
    yh = ABF.alf(16 * 512, "yh").rearrange("p (c t) -> p c t", t=512)
    sc_a = ABF.alf(512, "sc_a")
    sc_g = ABF.alf(512, "sc_g")


    def emit_scan_half(half):
        t0 = half * 512
        b.wait("tensor", "dve", b.cnt["dve"])
        for n in range(16):
            b.op("tensor", lambda t, n=n, t0=t0: t.matmul(
                ps_x[:, 0:512], lhsT=sb_ohbc[0:16, n * 128:(n + 1) * 128],
                rhs=sb_sqacc[0:16, t0:t0 + 512], start=True, stop=True), inc="pe")
            b.wait("vector", "pe", b.cnt["pe"])
            b.op("vector", lambda v, n=n: v.tensor_copy(out=rbb[:, n, :], in_=ps_x[:, 0:512]), inc="dve")
            b.wait("tensor", "dve", b.cnt["dve"])
        for dblk in range(16):
            for n in range(16):
                tau = n * 16 + dblk
                b.wait("scalar", "dve", b.cnt["dve"])
                b.op("scalar", lambda s, t0=t0, tau=tau: s.activation(
                    out=sc_a, in_=sb_ev[:, t0:t0 + 512], func=AF.Exp,
                    scale=sb_aarr[:, tau:tau + 1], bias=C_ZERO), inc="act")
                b.wait("vector", "act", b.cnt["act"])
                b.op("vector", lambda v, dblk=dblk, t0=t0, n=n, tau=tau: v.scalar_tensor_tensor(
                    out=ubf, in0=xcv(dblk)[:, t0:t0 + 512], scalar=sb_carr[:, tau:tau + 1],
                    in1=rbb[:, n, :], op0=OP.mult, op1=OP.mult), inc="dve")
                b.sw("vector", "dve")
                if half == 0:
                    b.op("vector", lambda v: v.tensor_tensor_scan(
                        out=sc_g, data0=sc_a, data1=ubf, initial=0.0,
                        op0=OP.mult, op1=OP.add), inc="dve")
                else:
                    b.op("vector", lambda v, tau=tau: v.tensor_tensor_scan(
                        out=sc_g, data0=sc_a, data1=ubf, initial=sb_carry[:, tau:tau + 1],
                        op0=OP.mult, op1=OP.add), inc="dve")
                b.sw("vector", "dve")
                b.op("vector", lambda v, tau=tau: v.tensor_copy(
                    out=sb_carry[:, tau:tau + 1], in_=sc_g[:, 511:512]), inc="dve")
                if half == 1:
                    b.op("vector", lambda v, tau=tau: v.tensor_copy(
                        out=sb_gfin[:, tau:tau + 1], in_=sc_g[:, 511:512]), inc="dve")
                if n == 0:
                    b.op("vector", lambda v, dblk=dblk: v.tensor_copy(out=yh[:, dblk, :], in_=sc_g), inc="dve")
                else:
                    b.op("vector", lambda v, dblk=dblk: v.tensor_tensor(
                        out=yh[:, dblk, :], in0=yh[:, dblk, :], in1=sc_g, op=OP.add), inc="dve")
                b.sw("vector", "dve")
        b.wait("sync", "dve", b.cnt["dve"])
        b.dma("sync", y_scr[:, half * 8192:(half + 1) * 8192], yh.rearrange("p c t -> p (c t)"))
        b.wait("vector", "d_sync", b.cnt["d_sync"])

    emit_scan_half(0)
    emit_scan_half(1)
    b.wait("scalar", "dve", b.cnt["dve"])
    b.op("scalar", lambda s: s.activation(out=sb_acc1[:, :], in_=sb_aarr[:, :], func=AF.Exp,
                                          scale=sb_stot[:, 0:1], bias=C_ZERO), inc="act")
    b.wait("sync", "act", b.cnt["act"])
    b.dma("sync", cc_in[:, 0:256], sb_acc1[:, :])
    b.dma("sync", cc_in[:, 256:512], sb_gfin[:, :])
    b.wait("gpsimd", "d_sync", b.cnt["d_sync"])
    if ENABLE_CC:
        b.op("gpsimd", lambda g: g.collective_compute(
            "AllGather", OP.bypass, replica_groups=[list(range(8))],
            ins=[cc_in.ap().opt()], outs=[cc_out.ap().opt()]), inc="cc", inc_by=1)
    else:
        b.op("gpsimd", lambda g: g.nop(), inc="cc", inc_by=1)
    b.wait("sync", "cc", b.cnt["cc"])
    ccv8 = cc_out.ap().rearrange("(r p) c -> p r c", p=128)
    b.dma("sync", sb_xst.ap().rearrange("p (r c) -> p r c", c=512), ccv8[:, 0:4, :])
    ccld = b.dma("sync", sb_xst2.ap().rearrange("p (r c) -> p r c", c=512), ccv8[:, 4:8, :])
    b.wait("vector", "d_sync", ccld)
    b.op("vector", lambda v: v.memset(sb_gin[:, :], 0.0), inc="dve")
    b.sw("vector", "dve")
    for r in range(7):
        ccsrc = (sb_xst if r < 4 else sb_xst2).ap().rearrange("p (r c) -> p r c", c=512)[:, r % 4, :]
        b.op("vector", lambda v, r=r, ccsrc=ccsrc: v.tensor_scalar(
            out=sb_acc1[:, :], in0=ccsrc[:, 0:256], scalar1=sb_selap[:, r:r + 1],
            scalar2=sb_sela1[:, r:r + 1], op0=OP.mult, op1=OP.add), inc="dve")
        b.op("vector", lambda v, r=r, ccsrc=ccsrc: v.tensor_scalar(
            out=sb_acc2[:, :], in0=ccsrc[:, 256:512], scalar1=sb_selb[:, r:r + 1],
            scalar2=None, op0=OP.mult), inc="dve")
        b.sw("vector", "dve")
        b.op("vector", lambda v: v.tensor_tensor(out=sb_gin[:, :], in0=sb_gin[:, :],
                                                 in1=sb_acc1[:, :], op=OP.mult), inc="dve")
        b.sw("vector", "dve")
        b.op("vector", lambda v: v.tensor_tensor(out=sb_gin[:, :], in0=sb_gin[:, :],
                                                 in1=sb_acc2[:, :], op=OP.add), inc="dve")
        b.sw("vector", "dve")
    if debug:
        b.wait("sync", "dve", b.cnt["dve"])
        b.dma("sync", dbg["gfin"][:, 0:256], sb_gfin[:, :])
        b.dma("sync", dbg["gfin"][:, 256:512], sb_gin[:, :])
        b.wait("sync", "d_sync", b.cnt["d_sync"])

    def emit_bout_half(half):
        t0 = half * 512
        yld = b.dma("sync", yh.rearrange("p c t -> p (c t)"), y_scr[:, half * 8192:(half + 1) * 8192])
        b.wait("scalar", "d_sync", yld)
        b.wait("vector", "d_sync", yld)
        for dblk in range(16):
            for n in range(16):
                tau = n * 16 + dblk
                b.wait("scalar", "dve", b.cnt["dve"])
                b.op("scalar", lambda s, tau=tau, t0=t0: s.activation(
                    out=acum, in_=sb_ev2[:, t0:t0 + 512], func=AF.Exp,
                    scale=sb_aarr[:, tau:tau + 1], bias=C_ZERO), inc="act")
                b.wait("vector", "act", b.cnt["act"])
                b.op("vector", lambda v, tau=tau, dblk=dblk: v.scalar_tensor_tensor(
                    out=yh[:, dblk, :], in0=acum, scalar=sb_gin[:, tau:tau + 1],
                    in1=yh[:, dblk, :], op0=OP.mult, op1=OP.add), inc="dve")
                b.sw("vector", "dve")
            b.op("vector", lambda v, dblk=dblk, t0=t0: v.tensor_tensor(
                out=yh[:, dblk, :], in0=yh[:, dblk, :], in1=svbf[:, dblk, t0:t0 + 512],
                op=OP.mult), inc="dve")
            b.sw("vector", "dve")
        if debug and half == 0:
            b.wait("sync", "dve", b.cnt["dve"])
            dl = b.dma("sync", dbg["yc"][:, :], yh.rearrange("p c t -> p (c t)"))
            b.wait("sync", "d_sync", dl)
        b.wait("tensor", "dve", b.cnt["dve"])
        for tb in range(4):
            gt = t0 + tb * 128
            for j in range(16):
                tgt = (ps_sc[:, j * 128:(j + 1) * 128] if j < 8
                       else ps_o[:, (j - 8) * 128:(j - 7) * 128])
                b.op("tensor", lambda t, j=j, tb=tb, tgt=tgt: t.transpose(
                    tgt, in_=yh[:, j, tb * 128:tb * 128 + 128], identity=sb_idf[:, :]),
                    inc=("pe" if j == 15 else None))
            b.wait("vector", "pe", b.cnt["pe"])
            b.op("vector", lambda v: v.tensor_copy(out=sb_xst2[:, 0:1024], in_=ps_sc[:, 0:1024]), inc="dve")
            b.op("vector", lambda v: v.tensor_copy(out=sb_xst2[:, 1024:2048], in_=ps_o[:, :]), inc="dve")
            b.wait("sync", "dve", b.cnt["dve"])
            b.dma("sync", ot_scr[gt:gt + 128, :], sb_xst2[:, :])
            b.wait("tensor", "d_sync", b.cnt["d_sync"])
            b.wait("tensor", "dve", b.cnt["dve"])

    emit_bout_half(0)
    emit_bout_half(1)
    ABF.pop()
    b.barrier()

    ABF.push()
    x3lnT = ABF.al3(16, SQ, "x3lnT")
    hact2 = ABF.al3(43, SQ, "hact2")
    if debug:
        for tb in range(8):
            dl = b.dma("sync", sb_xst[:, :], ot_scr[tb * 128:(tb + 1) * 128, :])
            b.wait("sync", "d_sync", dl)
            dl = b.dma("sync", dbg["ot2"][tb * 128:(tb + 1) * 128, :], sb_xst[:, :])
            b.wait("sync", "d_sync", dl)
    emit_ln(xs2, ot_scr, xs3, x3lnT, 0, 8)
    b.barrier()
    emit_ffn(x3lnT, hact2)
    ABF.pop()
    b.barrier()
    for tb in range(8):
        b.dma("sync", sb_xst[:, :], xs3[tb * 128:(tb + 1) * 128, :])
        db = b.dma("sync", sb_xst2[:, :], ot_scr[tb * 128:(tb + 1) * 128, :])
        b.wait("vector", "d_sync", db)
        b.op("vector", lambda v: v.tensor_tensor(out=sb_xst[:, :], in0=sb_xst[:, :],
                                                 in1=sb_xst2[:, :], op=OP.add), inc="dve")
        b.wait("sync", "dve", b.cnt["dve"])
        b.dma("sync", y_out[tb * 128:(tb + 1) * 128, :], sb_xst[:, :])
        b.wait("vector", "d_sync", b.cnt["d_sync"])
    b.wait("sync", "d_sync", b.cnt["d_sync"])
    b.wait("gpsimd", "d_gpsimd", b.cnt["d_gpsimd"])
    b.barrier()
    SELF_CLEAN = False
    if SELF_CLEAN:
        for e in ENGINES:
            if e != "gpsimd":
                b.op(e, lambda en: en.nop(), inc="pool")
        b.wait("gpsimd", "pool", b.cnt["pool"])
        b.op("gpsimd", lambda g: g.sem_clear(range(155, 256)))

    with es:
        with nc.Block() as block:
            for eng in ENGINES:
                def make_body(eng):
                    def body(e):
                        for f in b.q[eng]:
                            f(e)
                    return body
                getattr(block, eng)(make_body(eng))
    return nc


_PROG_CACHE = {}


def _get_prog(debug=False):
    if debug not in _PROG_CACHE:
        _PROG_CACHE[debug] = build_program(debug)
    return _PROG_CACHE[debug]


def host_inputs(inputs):
    x = np.asarray(inputs["L0_attn_a_hidden_states"], np.float32)
    wqkv = np.ascontiguousarray(np.asarray(inputs["L0_attn_a_Wqkv"], np.float32))
    qnw = np.asarray(inputs["L0_attn_a_q_norm_w"], np.float32)
    knw = np.asarray(inputs["L0_attn_a_k_norm_w"], np.float32)
    wo = np.ascontiguousarray(np.asarray(inputs["L0_attn_a_Wo"], np.float32))
    w_in = np.asarray(inputs["L0_attn_b_W_in"], np.float32)
    b_in = np.asarray(inputs["L0_attn_b_b_in"], np.float32)
    a_log = np.asarray(inputs["L0_attn_b_A_log"], np.float32)
    b_param = np.asarray(inputs["L0_attn_b_B_param"], np.float32)
    c_param = np.asarray(inputs["L0_attn_b_C_param"], np.float32)
    conv_w = np.asarray(inputs["L0_attn_b_conv_w"], np.float32)
    lut = np.asarray(inputs["L0_ffn_lut"], np.float32)
    gidxT = np.ascontiguousarray(np.asarray(inputs["L0_ffn_gate_idx"], np.int32).T)
    uidxT = np.ascontiguousarray(np.asarray(inputs["L0_ffn_up_idx"], np.int32).T)
    didxT = np.ascontiguousarray(np.asarray(inputs["L0_ffn_down_idx"], np.int32).T)

    inv_freq = 1.0 / THETA ** (np.arange(0, HD, 2, dtype=np.float32) / HD)
    aarr = np.zeros((128, 256), np.float32)
    carr = np.zeros((128, 256), np.float32)
    for n in range(16):
        for dblk in range(16):
            tau = n * 16 + dblk
            aarr[:, tau] = a_log[dblk * 128:(dblk + 1) * 128, n]
            carr[:, tau] = c_param[dblk * 128:(dblk + 1) * 128, n]
    convw = np.zeros((128, 64), np.float32)
    for dblk in range(16):
        for k in range(4):
            convw[:, dblk * 4 + k] = conv_w[dblk * 128:(dblk + 1) * 128, 0, k]

    per_core = []
    for c in range(8):
        bidx, j = c // 4, c % 4
        t0 = j * 1024
        xo = np.ascontiguousarray(x[bidx, t0:t0 + 1024])
        xh = (np.zeros((1024, H), np.float32) if j == 0
              else np.ascontiguousarray(x[bidx, t0 - 1024:t0]))
        pos = np.arange(t0 - 1024, t0 + 1024, dtype=np.float32)
        fr = pos[None, :] * inv_freq[:, None]
        _c = np.cos(fr).astype(np.float32)
        _s = np.sin(fr).astype(np.float32)
        cosw = np.concatenate([_c, _c], 0)
        sinw = np.concatenate([-_s, _s], 0)
        mask = np.zeros((8, 128, 1152), np.int8)
        for qb in range(8):
            gq = t0 + qb * 128 + np.arange(128)
            gk = (t0 - 1024) + qb * 128 + np.arange(1152)
            d = gq[:, None] - gk[None, :]
            mask[qb] = ((d >= 0) & (d <= 1023) & (gk[None, :] >= 0)).astype(np.int8)
        sela1 = np.zeros((128, 8), np.float32)
        selap = np.zeros((128, 8), np.float32)
        selb = np.zeros((128, 8), np.float32)
        bstart = bidx * 4
        for r in range(7):
            if bstart <= r < c:
                selb[:, r] = 1.0
                selap[:, r] = 0.0 if r == bstart else 1.0
            else:
                sela1[:, r] = 1.0
        seltl = np.zeros((128, 8), np.float32)
        if j > 0:
            seltl[:, c - 1] = 1.0
        per_core.append({
            "x_own": xo, "x_halo": xh, "wqkv": wqkv, "wo": wo,
            "win": np.ascontiguousarray(w_in[:, 0:4096]),
            "win_bt": np.ascontiguousarray(w_in[:, 4096:4112]),
            "win_dl": np.ascontiguousarray(w_in[:, 4112:4113]),
            "bias_v": np.ascontiguousarray(b_in[0:2048].reshape(16, 128).T),
            "bias_xc": np.ascontiguousarray(b_in[2048:4096].reshape(16, 128).T),
            "bias_bt": np.ascontiguousarray(b_in[4096:4112].reshape(16, 1)),
            "bias_dl": np.ascontiguousarray(b_in[4112:4113].reshape(1, 1)),
            "gq": np.ascontiguousarray(qnw.reshape(16, 128).T),
            "gk": np.ascontiguousarray(knw.reshape(16, 128).T),
            "cosw": cosw, "sinw": sinw, "maskmul": mask,
            "aarr": aarr, "carr": carr, "convw": convw,
            "bparam": np.ascontiguousarray(b_param.reshape(16, 1)),
            "lutv": np.ascontiguousarray(lut.reshape(1, 16)),
            "gidxT": np.ascontiguousarray(gidxT[c * 256:(c + 1) * 256]),
            "uidxT": np.ascontiguousarray(uidxT[c * 256:(c + 1) * 256]),
            "didxT": np.ascontiguousarray(didxT[c * 688:(c + 1) * 688]),
            "sel_a1": sela1, "sel_ap": selap, "sel_b": selb, "sel_tl": seltl,
        })
    return per_core


def kernel(**inputs):
    per_core = host_inputs(inputs)
    nc = _get_prog(False)
    res = run_bass_kernel_spmd(nc, per_core, core_ids=list(range(8)))
    out = np.zeros((2, 4096, H), np.float32)
    for c in range(8):
        bidx, j = c // 4, c % 4
        out[bidx, j * 1024:(j + 1) * 1024] = res.results[c]["y_out"]
    return out
